# version 06
# baseline (speedup 1.0000x reference)
import math
from contextlib import ExitStack

import numpy as np
import ml_dtypes

import concourse.bass as bass
import concourse.mybir as mybir
from concourse.bass import DynSlice
from concourse.bass_utils import run_bass_kernel_spmd

F32 = mybir.dt.float32
I32 = mybir.dt.int32
U8 = mybir.dt.uint8
FP8 = mybir.dt.float8e5
AF = mybir.ActivationFunctionType
ALU = mybir.AluOpType
AX = mybir.AxisListType

B, S, D, H, DK, DV, DFF = 4, 4096, 1024, 8, 128, 128, 4096
TPC = S // 2
NT = TPC // 128
C = 128
NCH = S // C
HOWN = H // 2
GRP = 1024
NTG = GRP // 128
NG = TPC // GRP
SCALE = DK ** -0.5

LN2 = np.float64(np.log(2.0))
LN2_HI = float(np.float32(LN2))
LN2_LO = float(np.float32(LN2 - np.float64(np.float32(LN2))))
INV_LN2 = float(np.float32(1.0 / LN2))

ENGS = ("sync", "tensor", "vector", "scalar", "gpsimd")


class StopEmission(Exception):
    pass


def _ckpt(tag):
    import os
    if os.environ.get("KSTOP") == tag:
        raise StopEmission(tag)


def _f32(x):
    return float(np.float32(x))


class Prog:
    def __init__(self, nc):
        self.nc = nc
        self.ops = {e: [] for e in ENGS}
        self.sems = {}
        self.cnt = {}
        self.ph_k = 0

    def sem(self, name):
        if name not in self.sems:
            self.sems[name] = self.nc.alloc_semaphore(name=name)
            self.cnt[name] = 0
        return self.sems[name]

    def val(self, name):
        self.sem(name)
        return self.cnt[name]

    def emit(self, eng, fn, waits=(), inc=None, inc_by=1):
        waits = tuple((s, v) for s, v in waits if callable(v) or v > 0)
        for s, v in waits:
            self.sem(s)
            if not callable(v):
                assert v <= self.cnt[s], (eng, s, v, self.cnt[s])
        if inc is not None:
            self.sem(inc)
            self.cnt[inc] += inc_by
        sems = self.sems

        def run(e, waits=waits, fn=fn, inc=inc, inc_by=inc_by):
            for s, v in waits:
                e.wait_ge(sems[s], v() if callable(v) else v)
            inst = fn(e)
            if inc is not None:
                if inst is None:
                    e.nop().then_inc(sems[inc], inc_by)
                else:
                    inst.then_inc(sems[inc], inc_by)

        self.ops[eng].append(run)
        return self.cnt[inc] if inc is not None else None

    def dma_pair(self, eng, oe, ie, oo, io, conds, waits=(), inc="dmisc"):
        def f1(e):
            inv, pv = conds()
            o = oe() if callable(oe) else oe
            i_ = ie() if callable(ie) else ie
            return e.dma_start(out=o, in_=i_, cond=inv)

        def f2(e):
            inv, pv = conds()
            o = oo() if callable(oo) else oo
            i_ = io() if callable(io) else io
            return e.dma_start(out=o, in_=i_, cond=pv)

        self.emit(eng, f1, waits=waits, inc=inc, inc_by=16)
        return self.emit(eng, f2, waits=(), inc=inc, inc_by=16)

    def idma_gather(self, out, dram_ap, idx_col, waits=(), inc="dmisc"):
        from concourse.bass import IndirectOffsetOnAxis

        def fn(e):
            return e.indirect_dma_start(
                out=out, out_offset=None, in_=dram_ap,
                in_offset=IndirectOffsetOnAxis(idx_col, 0))

        return self.emit("gpsimd", fn, waits=waits, inc=inc, inc_by=16)

    def idma_scatter(self, dram_ap, idx_col, in_, waits=(), inc="dmisc"):
        from concourse.bass import IndirectOffsetOnAxis

        def fn(e):
            return e.indirect_dma_start(
                out=dram_ap, out_offset=IndirectOffsetOnAxis(idx_col, 0),
                in_=in_, in_offset=None)

        return self.emit("gpsimd", fn, waits=waits, inc=inc, inc_by=16)

    def dma(self, eng, out, in_, waits=(), inc="dmisc"):
        def fn(e, out=out, in_=in_):
            o = out() if callable(out) else out
            i_ = in_() if callable(in_) else in_
            return e.dma_start(out=o, in_=i_)
        return self.emit(eng, fn, waits=waits, inc=inc, inc_by=16)

    def all_sync(self, eng_waits=None):
        self.ph_k += 1
        tgt = 5 * self.ph_k
        for e in ENGS:
            w = tuple((eng_waits or {}).get(e, ()))
            self.emit(e, lambda _e: None, waits=w, inc="ph")
        assert self.cnt["ph"] == tgt
        for e in ENGS:
            self.emit(e, lambda _e, t=tgt: _e.wait_ge(self.sems["ph"], t))
        return tgt


class K:
    pass


def build_program():
    nc = bass.Bass("TRN2", target_bir_lowering=False, debug=False,
                   detect_race_conditions=False)
    p = Prog(nc)
    k = K()
    k.nc, k.p = nc, p
    k.root = ExitStack()

    _names = {}

    def mksb(ctx):
        def sbf(name, shape, dt=F32):
            n = _names.get(name, 0)
            _names[name] = n + 1
            if n:
                name = f"{name}__{n}"
            return ctx.enter_context(nc.sbuf_tensor(name, shape, dt))
        return sbf

    sb = mksb(k.root)

    din = [
        ("x_in", [TPC, D], F32), ("wqkvg", [D, 4 * D], F32),
        ("wba", [D, 16], F32), ("alog_in", [1, 8], F32),
        ("dtb_in", [1, 8], F32),
        ("onw", [1, DV], F32), ("wo", [D, D], F32),
        ("wfc", [D, DFF], F32), ("wproj", [DFF, D], F32),
        ("w1th", [512, DFF], F32), ("b1r", [32, 128], F32),
        ("w2th", [2048, D], F32), ("b2r", [1, D], F32),
        ("cvec", [1, D], F32), ("wmod", [D, 6 * D], F32),
        ("mask8", [TPC, D], U8), ("ident", [128, 128], F32),
        ("m_incl_ti", [128, 128], F32), ("m_strict_ti", [128, 128], F32),
        ("m_incl_it", [128, 128], F32), ("m_strict_it", [128, 128], F32),
        ("parin", [1, 1], I32),
        ("noncein", [1, 1], I32),
        ("sel127in", [128, 128], F32),
        ("selhin", [8, 4, 128], F32),
        ("onesin", [1, 4, 128], F32),
        ("idx_sc", [128, 128], mybir.dt.uint32),
        ("idx_dn", [128, 128], mybir.dt.uint32),
        ("idx_bg", [128, 64], mybir.dt.uint32),
        ("idx_wf", [128, 20], mybir.dt.uint32),
    ]
    A = {}
    for nm, shp, dt in din:
        A[nm] = nc.dram_tensor(nm, shp, dt, kind="ExternalInput").ap()
    A["out"] = nc.dram_tensor("out", [TPC, D], F32, kind="ExternalOutput").ap()
    shr = dict(kind="Internal", addr_space="Shared")
    A["skq"] = nc.dram_tensor("skq", [H, S, DK], F32, **shr).ap()
    A["skk"] = nc.dram_tensor("skk", [H, S, DK], F32, **shr).ap()
    A["skv"] = nc.dram_tensor("skv", [H, S, DV], F32, **shr).ap()
    A["skbg"] = nc.dram_tensor("skbg", [2, 2, TPC, 8], F32, **shr).ap()
    A["sko"] = nc.dram_tensor("sko", [H, S, DV], F32, **shr).ap()
    A["w18s"] = nc.dram_tensor("w18s", [D, DFF], FP8, **shr).ap()
    A["w28s"] = nc.dram_tensor("w28s", [DFF, D], FP8, **shr).ap()
    A["flags"] = nc.dram_tensor("flags", [2, 16], I32, **shr).ap()
    A["sc_h"] = nc.dram_tensor("sc_h", [TPC, D], F32, kind="Internal").ap()
    A["sc_gate"] = nc.dram_tensor("sc_gate", [TPC, D], F32,
                                  kind="Internal").ap()
    A["sc_x1"] = nc.dram_tensor("sc_x1", [TPC, D], F32, kind="Internal").ap()
    A["sc_x2"] = nc.dram_tensor("sc_x2", [TPC, D], F32, kind="Internal").ap()
    k.A = A

    k.bias_tile = sb("biases", [128, 48])
    k.bias_vals = {}

    def bias_ap(val, pn=128):
        v = _f32(val)
        if v not in k.bias_vals:
            assert not getattr(k, "bias_frozen", False), (
                "bias value registered after freeze", v)
            idx = len(k.bias_vals)
            assert idx < 48
            k.bias_vals[v] = idx
            p.emit("gpsimd",
                   lambda e, vv=v, ii=idx: e.memset(
                       k.bias_tile[:, ii:ii + 1], vv), inc="binit")
        return k.bias_tile[0:pn, k.bias_vals[v]:k.bias_vals[v] + 1]

    k.bias_ap = bias_ap
    k.banks = [k.root.enter_context(nc.psum_tensor(f"bank{i}", [128, 512], F32))
               for i in range(8)]
    k.pars = sb("pars", [1, 1], I32)
    k.ident = sb("ident_sb", [128, 128])
    k.masks = {}
    k.ones = sb("ones128", [128, 128])
    k.onw_bc = sb("onw_bc", [128, DV])
    k.abias = sb("abias", [128, 16])
    k.stat = sb("stat", [128, 12])
    k.l2q = sb("l2q", [128, 24])
    k.sq2 = sb("sq2", [128, 1024])
    k.q_Y = k.sq2[:, 0:512]
    k.q_A = k.sq2[:, 512:1024]
    k.q_B = sb("q_B", [128, 512])
    k.q_C = sb("q_C", [128, 512])
    k.q_D = sb("q_D", [128, 512])
    k.q_E = sb("q_E", [128, 512])
    k.q_F = sb("q_F", [128, 512])
    k.q_I1 = sb("q_I1", [128, 512], I32)
    k.q_I2 = sb("q_I2", [128, 512], I32)
    k.sp_scr = sb("sp_scr", [128, 16])
    k.gapt = sb("gapt", [128, 64])

    def vgap(n=1):
        for _ in range(n):
            p.emit("vector", lambda e: e.tensor_scalar(
                out=k.gapt[:], in0=k.gapt[:], scalar1=0.0, scalar2=None,
                op0=ALU.mult), inc="vv")
    k.vgap = vgap
    k.bc = {nm: sb("bc_" + nm, [128, D]) for nm in
            ("sa", "ca1", "sbv", "cb1", "ga", "gb")}

    p.dma("sync", k.pars[:], A["parin"][:, :], inc="ld_par")
    k.nonces = sb("nonces", [1, 1], I32)
    p.dma("sync", k.nonces[:], A["noncein"][:, :], inc="ld_par")
    p.dma("sync", k.ident[:], A["ident"][:, :], inc="lde")
    k.onw_row = sb("onw_row", [1, DV])
    p.dma("sync", k.onw_row[:], A["onw"][:, :], inc="lde")
    k.alog = sb("alog_sb", [1, 8])
    p.dma("sync", k.alog[:], A["alog_in"][:, :], inc="lde")
    k.dtb = sb("dtb_sb", [1, 8])
    p.dma("sync", k.dtb[:], A["dtb_in"][:, :], inc="lde")
    k.idx_sc = sb("idx_sc_sb", [128, 128], mybir.dt.uint32)
    p.dma("sync", k.idx_sc[:], A["idx_sc"][:, :], inc="ldix")
    k.idx_dn = sb("idx_dn_sb", [128, 128], mybir.dt.uint32)
    p.dma("sync", k.idx_dn[:], A["idx_dn"][:, :], inc="ldix")
    k.idx_bg = sb("idx_bg_sb", [128, 64], mybir.dt.uint32)
    p.dma("sync", k.idx_bg[:], A["idx_bg"][:, :], inc="ldix")
    k.idx_wf = sb("idx_wf_sb", [128, 20], mybir.dt.uint32)
    p.dma("sync", k.idx_wf[:], A["idx_wf"][:, :], inc="ldix")
    k.sel127 = sb("sel127", [128, 128])
    p.dma("sync", k.sel127[:], A["sel127in"][:, :], inc="ldsel")
    k.sel_h = sb("sel_h", [8, 4, 128])
    p.dma("sync", k.sel_h[:], A["selhin"][:, :, :], inc="ldsel")
    k.p0span = ExitStack()
    p0sb = mksb(k.p0span)
    k.cvec = p0sb("cvec_sb", [1, D])
    p.dma("sync", k.cvec[:], A["cvec"][:, :], inc="lde")
    k.modrow = p0sb("modrow", [1, 6 * D])
    k.garow = p0sb("garow", [1, D])
    k.gbrow = p0sb("gbrow", [1, D])
    p.emit("gpsimd", lambda e: e.memset(k.ones[:], 1.0), inc="binit")
    c2a = quant_consts(2, -14)[1]
    c2b = quant_consts(7, None)[1]
    for _bv in (0.0, 1.0, 1e-5, 1e-6, -100.0, math.log(SCALE),
                c2a, -c2a, c2b, -c2b):
        bias_ap(_bv)
    k.bias_frozen = True
    k.BINIT0 = p.val("binit")
    k.LD0 = lambda: p.cnt["ld0"]

    k.pv = {}

    def parity(eng):
        if eng not in k.pv:
            slot = {}

            def f(e, slot=slot):
                r = e.alloc_register(f"preg_{eng}")
                e.reg_load(r, k.pars[0:1, 0:1])
                slot["v"] = e.snap(r, min_val=0, max_val=1)
                slot["inv"] = 1 - slot["v"]
                slot["v2"] = 1 - slot["inv"]

            p.emit(eng, f, waits=(("ld_par", 32),))
            k.pv[eng] = slot
        return lambda slot=k.pv[eng]: slot["v"]

    k.parity = parity
    gp_pv = parity("gpsimd")

    def pair_barrier(round_idx, waits_gp=()):
        def f(e):
            nr = e.alloc_register(f"nreg{round_idx}")
            e.reg_load(nr, k.nonces[0:1, 0:1])
            tok = e.snap(nr, min_val=1, max_val=1 << 30) + (round_idx + 1)
            e.store(A["flags"][DynSlice(gp_pv(), 1),
                               round_idx:round_idx + 1], tok)
            with e.register(f"barf{round_idx}") as fr, \
                    e.register(f"barc{round_idx}") as cr:
                e.reg_mov(cr, 1)
                with e.While(cr):
                    e.load(fr, A["flags"][DynSlice(1 - gp_pv(), 1),
                                          round_idx:round_idx + 1])
                    e.reg_alu(cr, fr, tok, op=ALU.not_equal)

        return p.emit("gpsimd", f,
                      waits=tuple(waits_gp) + (("ld_par", 32),),
                      inc="bar")

    k.pair_barrier = pair_barrier
    k.mksb = mksb

    try:
        _run_phases(nc, p, k)
    except StopEmission as ex:
        print("emission stopped at", ex)

    with nc.Block() as block:
        @block.sync
        def _(e):
            for f in p.ops["sync"]:
                f(e)

        @block.tensor
        def _(e):
            for f in p.ops["tensor"]:
                f(e)

        @block.vector
        def _(e):
            for f in p.ops["vector"]:
                f(e)

        @block.scalar
        def _(e):
            for f in p.ops["scalar"]:
                f(e)

        @block.gpsimd
        def _(e):
            for f in p.ops["gpsimd"]:
                f(e)

    try:
        k.root.close()
    except Exception:
        pass
    return nc


def _run_phases(nc, p, k):
    A = k.A
    mksb = k.mksb
    pair_barrier = k.pair_barrier
    ph0 = ExitStack()
    _ckpt("pre_p0")
    emit_p0(k, mksb(ph0))
    _ckpt("post_p0")
    p.all_sync()
    ph0.close()
    k.p0span.close()

    ph1 = ExitStack()
    k._ph1_stack = ph1
    _ckpt("pre_p1")
    emit_p1(k, mksb(ph1))
    _ckpt("post_p1")
    pair_barrier(0, waits_gp=(("gsc", p.val("gsc")), ("wfs", p.val("wfs"))))
    k.BAR0 = p.val("bar")
    p.all_sync(eng_waits={"gpsimd": [("bar", k.BAR0)]})
    ph1.close()

    k.tailspan = ExitStack()
    tsb = mksb(k.tailspan)
    for nm in ("m_incl_ti", "m_strict_ti", "m_incl_it", "m_strict_it"):
        k.masks[nm] = tsb(nm + "_sb", [128, 128])
        p.dma("sync", k.masks[nm][:], A[nm][:, :], inc="ld0")
    k.b1c = tsb("b1c", [128, 32])
    k.b2bc = tsb("b2bc", [128, D])

    ph2 = ExitStack()
    _ckpt("pre_b1b2")
    emit_b1b2(k, mksb(ph2))
    _ckpt("pre_p2")
    emit_p2(k, mksb(ph2))
    _ckpt("post_p2")
    pair_barrier(1, waits_gp=(("gpd", p.val("gpd")),))
    k.BAR1 = p.val("bar")
    p.all_sync(eng_waits={"gpsimd": [("bar", k.BAR1)]})
    ph2.close()

    for g in range(NG):
        span45 = ExitStack()
        x8 = mksb(span45)("x8grp", [128, NTG, 8, 128], FP8)
        ystack = ExitStack()
        yacc = [mksb(ystack)(f"yacc{i}", [128, D]) for i in range(NTG)]
        mstack = ExitStack()
        mfm = mksb(mstack)("mfm", [128, 8, GRP])

        ph3 = ExitStack()
        emit_p3(k, mksb(ph3), g, mfm)
        p.all_sync()
        ph3.close()

        ph3b = ExitStack()
        emit_attnffn(k, mksb(ph3b), g, mfm, yacc)
        p.all_sync()
        ph3b.close()
        mstack.close()

        ph4a = ExitStack()
        emit_p4a(k, mksb(ph4a), g, yacc, x8)
        p.all_sync()
        ph4a.close()
        ystack.close()

        ph4b = ExitStack()
        emit_p4b(k, mksb(ph4b), g, x8)
        p.all_sync(eng_waits={"gpsimd": [("outd", p.val("outd"))]})
        ph4b.close()
        span45.close()

    k.tailspan.close()
    p.emit("sync", lambda e: e.wait_ge(p.sems["outd"], p.cnt["outd"]))



def quant_consts(M, clamp_e):
    c_shift = -(64.0 + M)
    c2 = np.float32(c_shift * np.float64(LN2_HI))
    c1 = np.float32(np.float64(c_shift) * LN2 - np.float64(c2))
    ecl = 64.0 + clamp_e if clamp_e is not None else -1e9
    return float(c1), float(c2), float(ecl)


def emit_fq_chain(k, src_ap, dst_ap, M, clamp_e, wv=(), pn=128, n=512,
                  snap=False):
    p, bias_ap = k.p, k.bias_ap
    c1, c2, ecl = quant_consts(M, clamp_e)
    Aa = k.q_A[0:pn, 0:n]
    Bb = k.q_B[0:pn, 0:n]
    Cc = k.q_C[0:pn, 0:n]
    Dd = k.q_D[0:pn, 0:n]
    Ee = k.q_E[0:pn, 0:n]
    I1 = k.q_I1[0:pn, 0:n]
    I2 = k.q_I2[0:pn, 0:n]
    p.emit("scalar", lambda e: e.activation(out=Aa, in_=src_ap, func=AF.Abs),
           waits=wv + (("vv", p.val("vv")),), inc="sc")
    p.emit("scalar", lambda e: e.activation(out=Ee, in_=src_ap, func=AF.Sign),
           inc="sc")
    sc0 = p.val("sc")
    p.emit("vector", lambda e: e.tensor_scalar(
        out=Aa, in0=Aa, scalar1=_f32(2.0 ** -40), scalar2=None, op0=ALU.max),
        waits=(("sc", sc0),), inc="vv")
    vv0 = p.val("vv")
    p.emit("scalar", lambda e: e.activation(out=Bb, in_=Aa, func=AF.Ln),
           waits=(("vv", vv0),), inc="sc")
    sc1 = p.val("sc")
    p.emit("vector", lambda e: e.tensor_scalar(
        out=I1, in0=Bb, scalar1=INV_LN2, scalar2=63.5, op0=ALU.mult,
        op1=ALU.add), waits=(("sc", sc1),), inc="vv")
    p.emit("vector", lambda e: e.tensor_scalar(
        out=Bb, in0=I1, scalar1=_f32(ecl), scalar2=None, op0=ALU.max),
        inc="vv")
    p.emit("vector", lambda e: e.tensor_scalar(
        out=Cc, in0=Bb, scalar1=LN2_LO, scalar2=c1, op0=ALU.mult,
        op1=ALU.add), inc="vv")
    p.emit("vector", lambda e: e.scalar_tensor_tensor(
        out=Bb, in0=Bb, scalar=LN2_HI, in1=Cc, op0=ALU.mult, op1=ALU.add),
        inc="vv")
    vv1 = p.val("vv")
    p.emit("scalar", lambda e: e.activation(
        out=Cc, in_=Bb, func=AF.Exp, bias=bias_ap(c2, pn), scale=1.0),
        waits=(("vv", vv1),), inc="sc")
    p.emit("scalar", lambda e: e.activation(
        out=Dd, in_=Bb, func=AF.Exp, bias=bias_ap(-c2, pn), scale=-1.0),
        inc="sc")
    sc2 = p.val("sc")
    if snap:
        p.emit("vector", lambda e: e.tensor_tensor(
            out=Bb, in0=Aa, in1=Dd, op=ALU.mult),
            waits=(("sc", sc2),), inc="vv")
        p.emit("vector", lambda e: e.tensor_scalar(
            out=I2, in0=Bb, scalar1=512.0, scalar2=None, op0=ALU.mult),
            inc="vv")
        p.emit("vector", lambda e: e.tensor_scalar(
            out=I1, in0=I2, scalar1=_f32(1.0 / 512.0), scalar2=None,
            op0=ALU.mult), inc="vv")
    else:
        p.emit("vector", lambda e: e.tensor_tensor(
            out=I1, in0=Aa, in1=Dd, op=ALU.mult),
            waits=(("sc", sc2),), inc="vv")
    p.emit("vector", lambda e: e.tensor_tensor(
        out=Bb, in0=I1, in1=Cc, op=ALU.mult), inc="vv")
    p.emit("vector", lambda e: e.tensor_tensor(
        out=dst_ap, in0=Bb, in1=Ee, op=ALU.mult), inc="vv")
    return p.val("vv")


def emit_rmfq_chain(k, psum_ap, b1col, dst8_ap, wv=(), pn=128, n=512):
    p, bias_ap = k.p, k.bias_ap
    c1r, c2r, _ = quant_consts(7, None)
    c1f, c2f, eclf = quant_consts(2, -14)
    Y = k.q_Y[0:pn, 0:n]
    Aa = k.q_A[0:pn, 0:n]
    Bb = k.q_B[0:pn, 0:n]
    Cc = k.q_C[0:pn, 0:n]
    Dd = k.q_D[0:pn, 0:n]
    Ee = k.q_E[0:pn, 0:n]
    Ff = k.q_F[0:pn, 0:n]
    I1 = k.q_I1[0:pn, 0:n]
    I2 = k.q_I2[0:pn, 0:n]
    p.emit("vector", lambda e: e.tensor_scalar(
        out=Y, in0=psum_ap, scalar1=b1col, scalar2=None, op0=ALU.add),
        waits=wv + (("sc", p.val("sc")),), inc="vv")
    vvy = p.val("vv")
    p.emit("scalar", lambda e: e.activation(out=Aa, in_=Y, func=AF.Abs),
           waits=(("vv", vvy),), inc="sc")
    sc0 = p.val("sc")
    p.emit("vector", lambda e: e.tensor_scalar(
        out=Ff, in0=Y, scalar1=0.0, scalar2=None, op0=ALU.is_gt),
        inc="vv")
    p.emit("vector", lambda e: e.tensor_scalar(
        out=Aa, in0=Aa, scalar1=_f32(2.0 ** -40), scalar2=None, op0=ALU.max),
        waits=(("sc", sc0),), inc="vv")
    vv0 = p.val("vv")
    p.emit("scalar", lambda e: e.activation(out=Bb, in_=Aa, func=AF.Ln),
           waits=(("vv", vv0),), inc="sc")
    sc1 = p.val("sc")
    p.emit("vector", lambda e: e.tensor_scalar(
        out=I1, in0=Bb, scalar1=INV_LN2, scalar2=63.5, op0=ALU.mult,
        op1=ALU.add), waits=(("sc", sc1),), inc="vv")
    p.emit("vector", lambda e: e.tensor_scalar(
        out=Cc, in0=I1, scalar1=LN2_LO, scalar2=c1r, op0=ALU.mult,
        op1=ALU.add), inc="vv")
    p.emit("vector", lambda e: e.scalar_tensor_tensor(
        out=Bb, in0=I1, scalar=LN2_HI, in1=Cc, op0=ALU.mult, op1=ALU.add),
        inc="vv")
    vv1 = p.val("vv")
    p.emit("scalar", lambda e: e.activation(
        out=Cc, in_=Bb, func=AF.Exp, bias=bias_ap(c2r), scale=1.0),
        waits=(("vv", vv1),), inc="sc")
    p.emit("scalar", lambda e: e.activation(
        out=Dd, in_=Bb, func=AF.Exp, bias=bias_ap(-c2r), scale=-1.0),
        inc="sc")
    sc2 = p.val("sc")
    p.emit("vector", lambda e: e.tensor_tensor(
        out=I1, in0=Aa, in1=Dd, op=ALU.mult), waits=(("sc", sc2),), inc="vv")
    p.emit("vector", lambda e: e.tensor_tensor(
        out=Aa, in0=I1, in1=Cc, op=ALU.mult), inc="vv")
    p.emit("vector", lambda e: e.tensor_scalar(
        out=Aa, in0=Aa, scalar1=_f32(2.0 ** -40), scalar2=None, op0=ALU.max),
        inc="vv")
    vv2 = p.val("vv")
    p.emit("scalar", lambda e: e.activation(out=Bb, in_=Aa, func=AF.Ln),
           waits=(("vv", vv2),), inc="sc")
    sc3 = p.val("sc")
    p.emit("vector", lambda e: e.tensor_scalar(
        out=I1, in0=Bb, scalar1=INV_LN2, scalar2=63.5, op0=ALU.mult,
        op1=ALU.add), waits=(("sc", sc3),), inc="vv")
    p.emit("vector", lambda e: e.tensor_scalar(
        out=Bb, in0=I1, scalar1=_f32(eclf), scalar2=None, op0=ALU.max),
        inc="vv")
    p.emit("vector", lambda e: e.tensor_scalar(
        out=Cc, in0=Bb, scalar1=LN2_LO, scalar2=c1f, op0=ALU.mult,
        op1=ALU.add), inc="vv")
    p.emit("vector", lambda e: e.scalar_tensor_tensor(
        out=Bb, in0=Bb, scalar=LN2_HI, in1=Cc, op0=ALU.mult, op1=ALU.add),
        inc="vv")
    vv3 = p.val("vv")
    p.emit("scalar", lambda e: e.activation(
        out=Cc, in_=Bb, func=AF.Exp, bias=bias_ap(c2f), scale=1.0),
        waits=(("vv", vv3),), inc="sc")
    p.emit("scalar", lambda e: e.activation(
        out=Dd, in_=Bb, func=AF.Exp, bias=bias_ap(-c2f), scale=-1.0),
        inc="sc")
    sc4 = p.val("sc")
    p.emit("vector", lambda e: e.tensor_tensor(
        out=Bb, in0=Aa, in1=Dd, op=ALU.mult), waits=(("sc", sc4),), inc="vv")
    p.emit("vector", lambda e: e.tensor_scalar(
        out=I2, in0=Bb, scalar1=512.0, scalar2=None, op0=ALU.mult), inc="vv")
    p.emit("vector", lambda e: e.tensor_scalar(
        out=I1, in0=I2, scalar1=_f32(1.0 / 512.0), scalar2=None,
        op0=ALU.mult), inc="vv")
    p.emit("vector", lambda e: e.tensor_tensor(
        out=Bb, in0=I1, in1=Cc, op=ALU.mult), inc="vv")
    p.emit("vector", lambda e: e.tensor_tensor(
        out=dst8_ap, in0=Bb, in1=Ff, op=ALU.mult), inc="vv")
    return p.val("vv")



def emit_p0(k, sb):
    p, banks, ident, ones, bias_ap = k.p, k.banks, k.ident, k.ones, k.bias_ap
    A = k.A
    silc = sb("silc", [1, D])
    p.emit("scalar",
           lambda e: e.activation(out=silc[:], in_=k.cvec[:], func=AF.Silu),
           waits=(("lde", 80), ("binit", k.BINIT0)), inc="sc")
    SCs = p.val("sc")
    silc_fm = sb("silc_fm", [128, 8])
    for kk in range(8):
        p.emit("tensor", lambda e, kk=kk: e.transpose(
            banks[kk % 2][:, 0:1], silc[0:1, kk * 128:(kk + 1) * 128],
            ident[0:1, 0:1]),
            waits=(("sc", SCs), ("vv", p.val("vv")), ("lde", 80)),
            inc="tm")
        p.emit("vector", lambda e, kk=kk: e.tensor_copy(
            silc_fm[:, kk:kk + 1], banks[kk % 2][:, 0:1]),
            waits=(("tm", p.val("tm")),), inc="vv")
    wbuf = [sb(f"wmodb{i}", [128, 8, 512]) for i in range(2)]
    reuse_tm = [0, 0]
    for gi in range(12):
        buf = wbuf[gi % 2]
        need = []
        for kk in range(8):
            v = p.dma("sync", buf[:, kk, :],
                      A["wmod"][kk * 128:(kk + 1) * 128,
                                gi * 512:(gi + 1) * 512],
                      waits=(("tm", reuse_tm[gi % 2]),),
                      inc="wml%d" % (gi % 2))
            need.append(v)
        pb = banks[2 + gi % 2]
        for kk in range(8):
            p.emit("tensor", lambda e, kk=kk, pb=pb, buf=buf: e.matmul(
                pb[0:1, 0:512], silc_fm[:, kk:kk + 1], buf[:, kk, :],
                start=(kk == 0), stop=(kk == 7)),
                waits=(("wml%d" % (gi % 2), need[-1]),
                       ("vv", p.val("vv"))), inc="tm")
        reuse_tm[gi % 2] = p.val("tm")
        p.emit("vector", lambda e, pb=pb, gi=gi: e.tensor_copy(
            k.modrow[0:1, gi * 512:(gi + 1) * 512], pb[0:1, 0:512]),
            waits=(("tm", p.val("tm")),), inc="vv")
    MOD = p.val("vv")

    mrow = {nm: k.modrow[0:1, i * D:(i + 1) * D] for i, nm in
            enumerate(("sa", "ca", "ga", "sbv", "cbv", "gbv"))}
    sgn = sb("sgnrow", [1, D])
    for nm, dst in (("ga", k.garow), ("gbv", k.gbrow)):
        src = mrow[nm]
        p.emit("vector", lambda e, s=src: e.tensor_scalar(
            out=sgn[:], in0=s, scalar1=0.0, scalar2=2.0, op0=ALU.is_ge,
            op1=ALU.mult), waits=(("vv", MOD),), inc="vv")
        p.emit("vector", lambda e: e.tensor_scalar(
            out=sgn[:], in0=sgn[:], scalar1=-1.0, scalar2=None, op0=ALU.add),
            inc="vv")
        p.emit("scalar", lambda e, s=src, d=dst: e.activation(
            out=d[:], in_=s, func=AF.Abs), waits=(("vv", MOD),), inc="sc")
        p.emit("vector", lambda e, d=dst: e.tensor_scalar(
            out=d[:], in0=d[:], scalar1=0.25, scalar2=None, op0=ALU.max),
            waits=(("sc", p.val("sc")),), inc="vv")
        p.emit("vector", lambda e, d=dst: e.tensor_tensor(
            out=d[:], in0=d[:], in1=sgn[:], op=ALU.mult), inc="vv")

    CL = p.val("vv")
    emit_bcast_tile(k, k.bc["sa"], k.modrow[0:1, 0:D], wv=MOD)
    emit_bcast_tile(k, k.bc["ca1"], k.modrow[0:1, D:2 * D], add_one=True,
                    wv=MOD)
    emit_bcast_tile(k, k.bc["sbv"], k.modrow[0:1, 3 * D:4 * D], wv=MOD)
    emit_bcast_tile(k, k.bc["cb1"], k.modrow[0:1, 4 * D:5 * D],
                    add_one=True, wv=MOD)
    emit_bcast_tile(k, k.bc["ga"], k.garow[0:1, :], wv=CL)
    emit_bcast_tile(k, k.bc["gb"], k.gbrow[0:1, :], wv=CL)

    expA = sb("expA", [1, 8])
    p.emit("scalar", lambda e: e.activation(
        out=expA[:], in_=k.alog[0:1, :], func=AF.Exp),
        waits=(("lde", 80),), inc="sc")
    SCe = p.val("sc")
    p.emit("tensor", lambda e: e.matmul(
        banks[6][:, 0:8], ones[0:1, :], expA[0:1, :], start=True, stop=True),
        waits=(("sc", SCe), ("binit", p.val("binit")), ("vv", p.val("vv"))),
        inc="tm")
    p.emit("vector", lambda e: e.tensor_scalar(
        out=k.abias[:, 0:8], in0=banks[6][:, 0:8], scalar1=-1.0,
        scalar2=None, op0=ALU.mult), waits=(("tm", p.val("tm")),), inc="vv")
    p.emit("tensor", lambda e: e.matmul(
        banks[6][:, 8:16], ones[0:1, :], k.dtb[0:1, 0:8],
        start=True, stop=True),
        waits=(("vv", p.val("vv")), ("lde", 80)), inc="tm")
    p.emit("vector", lambda e: e.tensor_copy(
        k.abias[:, 8:16], banks[6][:, 8:16]),
        waits=(("tm", p.val("tm")),), inc="vv")
    p.emit("tensor", lambda e: e.matmul(
        banks[7][:, 0:DV], ones[0:1, :], k.onw_row[0:1, :],
        start=True, stop=True),
        waits=(("lde", 80), ("vv", p.val("vv"))), inc="tm")
    p.emit("vector", lambda e: e.tensor_copy(k.onw_bc[:], banks[7][:, 0:DV]),
           waits=(("tm", p.val("tm")),), inc="vv")


def emit_bcast_tile(k, dst, src_row, add_one=False, wv=0):
    p, banks, ones = k.p, k.banks, k.ones
    for hh in range(2):
        pb = banks[4 + hh]
        p.emit("tensor", lambda e, pb=pb, hh=hh, s=src_row: e.matmul(
            pb[:, 0:512], ones[0:1, :], s[0:1, hh * 512:(hh + 1) * 512],
            start=True, stop=True),
            waits=(("vv", wv), ("vv", p.val("vv")), ("sc", p.val("sc")),
                   ("binit", p.val("binit"))),
            inc="tm")
        if add_one:
            p.emit("vector", lambda e, pb=pb, hh=hh, d=dst: e.tensor_scalar(
                out=d[:, hh * 512:(hh + 1) * 512], in0=pb[:, 0:512],
                scalar1=1.0, scalar2=None, op0=ALU.add),
                waits=(("tm", p.val("tm")),), inc="vv")
        else:
            p.emit("vector", lambda e, pb=pb, hh=hh, d=dst: e.tensor_copy(
                d[:, hh * 512:(hh + 1) * 512], pb[:, 0:512]),
                waits=(("tm", p.val("tm")),), inc="vv")



def emit_p1(k, sb):
    p, banks, ident, bias_ap = k.p, k.banks, k.ident, k.bias_ap
    A, masks, ones = k.A, k.masks, k.ones
    bc_sa = k.bc["sa"]
    bc_ca1 = k.bc["ca1"]
    tile_stack = ExitStack()
    sb = k.mksb(tile_stack)
    wq = [sb(f"wqkvg{kk}", [128, 4 * D]) for kk in range(8)]
    for kk in range(8):
        p.dma("sync", wq[kk][:], A["wqkvg"][kk * 128:(kk + 1) * 128, :],
              inc="wqld")
    wba = sb("wba_sb", [128, 8, 16])
    for kk in range(8):
        p.dma("sync", wba[:, kk, :], A["wba"][kk * 128:(kk + 1) * 128, :],
              inc="wqld")
    WQLD = p.val("wqld")

    xt = [sb("xt0", [128, D])] * 2
    hfm = [sb("hfm0", [128, 8, 128])] * 2
    qkv = sb("qkvt", [128, 3 * D])
    gate = sb("gatet", [128, D])
    bgA = sb("bgtA", [128, 8])
    bgB = sb("bgtB", [128, 8])
    stat, l2q = k.stat, k.l2q
    sqA = k.q_A

    xin_t = A["x_in"].rearrange("(n p) d -> n p d", p=128)
    schh = A["sc_h"].rearrange("(n p) d -> n p d", p=128)
    scgg = A["sc_gate"].rearrange("(n p) d -> n p d", p=128)

    store_wm = [0, 0]
    hstore_wm = [0, 0]
    qkv_gpd_wm = 0
    for i in range(NT):
        sl = 0
        xs, hf = xt[sl], hfm[sl]
        hh = xs
        xv = p.dma("sync", xs[:], xin_t[i, :, :],
                   waits=(("vv", store_wm[sl]),), inc="xld")
        p.emit("vector", lambda e, xs=xs: e.tensor_reduce(
            out=stat[:, 0:1], in_=xs[:], axis=AX.X, op=ALU.add),
            waits=(("xld", xv), ("sc", p.val("sc"))), inc="vv")
        p.emit("vector", lambda e, xs=xs: e.scalar_tensor_tensor(
            out=k.sq2[:, 0:512], in0=xs[:, 0:512], scalar=1.0, op0=ALU.mult,
            in1=xs[:, 0:512], op1=ALU.mult), inc="vv")
        p.emit("vector", lambda e, xs=xs: e.scalar_tensor_tensor(
            out=k.sq2[:, 512:1024], in0=xs[:, 512:1024], scalar=1.0,
            op0=ALU.mult, in1=xs[:, 512:1024], op1=ALU.mult), inc="vv")
        p.emit("vector", lambda e: e.tensor_reduce(
            out=stat[:, 1:2], in_=k.sq2[:], axis=AX.X, op=ALU.add),
            inc="vv")
        vvs0 = p.val("vv")
        p.emit("scalar", lambda e: e.activation(
            out=stat[:, 2:3], in_=stat[:, 0:1], func=AF.Identity,
            scale=_f32(1.0 / D)), waits=(("vv", vvs0),), inc="sc")
        p.emit("scalar", lambda e: e.activation(
            out=stat[:, 3:4], in_=stat[:, 2:3], func=AF.Square), inc="sc")
        scs0 = p.val("sc")
        p.emit("vector", lambda e: e.scalar_tensor_tensor(
            out=stat[:, 3:4], in0=stat[:, 1:2], scalar=_f32(1.0 / D),
            op0=ALU.mult, in1=stat[:, 3:4], op1=ALU.subtract),
            waits=(("sc", scs0),), inc="vv")
        vvs = p.val("vv")
        p.emit("scalar", lambda e: e.activation(
            out=stat[:, 4:5], in_=stat[:, 3:4], func=AF.Ln,
            bias=bias_ap(1e-5), scale=1.0), waits=(("vv", vvs),), inc="sc")
        p.emit("scalar", lambda e: e.activation(
            out=stat[:, 4:5], in_=stat[:, 4:5], func=AF.Exp,
            bias=bias_ap(0.0), scale=-0.5), inc="sc")
        scs = p.val("sc")
        p.emit("vector", lambda e, xs=xs, hh=hh: e.tensor_scalar(
            out=hh[:], in0=xs[:], scalar1=stat[:, 2:3], scalar2=stat[:, 4:5],
            op0=ALU.subtract, op1=ALU.mult),
            waits=(("sc", scs),), inc="vv")
        p.emit("vector", lambda e, hh=hh: e.tensor_tensor(
            out=hh[:], in0=hh[:], in1=bc_ca1[:], op=ALU.mult), inc="vv")
        p.emit("vector", lambda e, hh=hh: e.tensor_tensor(
            out=hh[:], in0=hh[:], in1=bc_sa[:], op=ALU.add), inc="vv")
        vvh = p.val("vv")
        hsv = p.dma("gpsimd", schh[i, :, :], hh[:], waits=(("vv", vvh),),
                    inc="ghd")
        hstore_wm[sl] = hsv
        p.emit("vector", lambda e, hh=hh: e.scalar_tensor_tensor(
            out=k.sq2[:, 0:512], in0=hh[:, 0:512], scalar=1.0, op0=ALU.mult,
            in1=hh[:, 0:512], op1=ALU.mult), inc="vv")
        p.emit("vector", lambda e, hh=hh: e.scalar_tensor_tensor(
            out=k.sq2[:, 512:1024], in0=hh[:, 512:1024], scalar=1.0,
            op0=ALU.mult, in1=hh[:, 512:1024], op1=ALU.mult), inc="vv")
        p.emit("vector", lambda e: e.tensor_reduce(
            out=stat[:, 5:6], in_=k.sq2[:], axis=AX.X, op=ALU.add),
            inc="vv")
        vvr = p.val("vv")
        p.emit("scalar", lambda e: e.activation(
            out=stat[:, 5:6], in_=stat[:, 5:6], func=AF.Ln,
            bias=bias_ap(1e-6), scale=_f32(1.0 / D)),
            waits=(("vv", vvr),), inc="sc")
        p.emit("scalar", lambda e: e.activation(
            out=stat[:, 5:6], in_=stat[:, 5:6], func=AF.Exp,
            bias=bias_ap(0.0), scale=-0.5), inc="sc")
        scr = p.val("sc")
        p.emit("vector", lambda e, hh=hh, xs=xs: e.tensor_scalar(
            out=xs[:], in0=hh[:], scalar1=stat[:, 5:6], scalar2=None,
            op0=ALU.mult), waits=(("sc", scr), ("ghd", hsv)), inc="vv")
        vvhs = p.val("vv")
        for kk in range(8):
            p.emit("tensor", lambda e, xs=xs, kk=kk: e.transpose(
                banks[kk % 2][:, 0:128], xs[:, kk * 128:(kk + 1) * 128],
                ident[:]),
                waits=(("vv", vvhs), ("vv", p.val("vv")),
                       ("sc", p.val("sc"))), inc="tm")
            p.emit("vector", lambda e, hf=hf, kk=kk: e.tensor_copy(
                hf[:, kk, :], banks[kk % 2][:, 0:128]),
                waits=(("tm", p.val("tm")),), inc="vv")
        vvt = p.val("vv")
        for kk in range(8):
            p.emit("tensor", lambda e, hf=hf, kk=kk: e.matmul(
                banks[6][:, 0:16], hf[:, kk, :], wba[:, kk, :],
                start=(kk == 0), stop=(kk == 7)),
                waits=(("vv", vvt), ("wqld", WQLD), ("sc", p.val("sc"))),
                inc="tm")
        tmba = p.val("tm")
        for gidx, bgX in ((0, bgA), (1, bgB)):
            p.emit("scalar", lambda e, gidx=gidx, bgX=bgX: e.activation(
                out=bgX[:, 0:4], in_=banks[6][:, gidx * 4:gidx * 4 + 4],
                func=AF.Sigmoid),
                waits=(("tm", tmba), ("gsc", qkv_gpd_wm)), inc="sc")
            p.emit("vector", lambda e, gidx=gidx, bgX=bgX: e.tensor_tensor(
                out=bgX[:, 4:8], in0=banks[6][:, 8 + gidx * 4:12 + gidx * 4],
                in1=k.abias[:, 8 + gidx * 4:12 + gidx * 4],
                op=ALU.add), waits=(("tm", tmba), ("gsc", qkv_gpd_wm)),
                inc="vv")
            vvb = p.val("vv")
            sp = k.sp_scr
            p.emit("scalar", lambda e, bgX=bgX, sp=sp: e.activation(
                out=sp[:, 0:4], in_=bgX[:, 4:8], func=AF.Abs),
                waits=(("vv", vvb),), inc="sc")
            p.emit("scalar", lambda e, sp=sp: e.activation(
                out=sp[:, 4:8], in_=sp[:, 0:4], func=AF.Exp, scale=-1.0),
                inc="sc")
            p.emit("scalar", lambda e, sp=sp: e.activation(
                out=sp[:, 8:12], in_=sp[:, 4:8], func=AF.Ln,
                bias=bias_ap(1.0, 128)), inc="sc")
            p.emit("scalar", lambda e, bgX=bgX, sp=sp: e.activation(
                out=sp[:, 12:16], in_=bgX[:, 4:8], func=AF.Relu),
                inc="sc")
            scsp = p.val("sc")
            p.emit("vector", lambda e, bgX=bgX, sp=sp: e.tensor_tensor(
                out=bgX[:, 4:8], in0=sp[:, 8:12], in1=sp[:, 12:16],
                op=ALU.add), waits=(("sc", scsp),), inc="vv")
            k.vgap(2)
            p.emit("vector", lambda e, gidx=gidx, bgX=bgX: e.tensor_tensor(
                out=bgX[:, 4:8], in0=bgX[:, 4:8],
                in1=k.abias[:, gidx * 4:gidx * 4 + 4],
                op=ALU.mult), inc="vv")
        vvbg = p.val("vv")
        skbg_flat = A["skbg"].rearrange("a b t c -> (a b t) c")
        for gidx, bgX in ((0, bgA), (1, bgB)):
            p.idma_scatter(skbg_flat, k.idx_bg[:, i * 2 + gidx:
                                               i * 2 + gidx + 1],
                           bgX[:], waits=(("vv", vvbg), ("ldix", 64),
                                          ("sc", p.val("sc"))),
                           inc="gsc")
        for cg in range(8):
            for kk in range(8):
                p.emit("tensor", lambda e, hf=hf, kk=kk, cg=cg: e.matmul(
                    banks[cg][:, 0:512], hf[:, kk, :],
                    wq[kk][:, cg * 512:(cg + 1) * 512],
                    start=(kk == 0), stop=(kk == 7)),
                    waits=(("vv", vvbg), ("sc", p.val("sc")),
                           ("vv", p.val("vv"))), inc="tm")
        tmq = p.val("tm")
        for cg in range(6):
            p.emit("scalar", lambda e, cg=cg: e.activation(
                out=qkv[:, cg * 512:(cg + 1) * 512], in_=banks[cg][:, 0:512],
                func=AF.Silu), waits=(("tm", tmq), ("gsc", qkv_gpd_wm)),
                inc="sc")
        for cg in range(2):
            p.emit("scalar", lambda e, cg=cg: e.activation(
                out=gate[:, cg * 512:(cg + 1) * 512],
                in_=banks[6 + cg][:, 0:512], func=AF.Silu),
                waits=(("tm", tmq), ("gsc", qkv_gpd_wm)), inc="sc")
        sce = p.val("sc")
        p.dma("gpsimd", scgg[i, :, :], gate[:], waits=(("sc", sce),),
              inc="ggt")
        for which, base, logb in (("q", 0, math.log(SCALE)), ("k", D, 0.0)):
            col = 8 if which == "q" else 16
            for half in range(2):
                p.emit("vector", lambda e, base=base, half=half:
                       e.scalar_tensor_tensor(
                           out=sqA[:],
                           in0=qkv[:, base + half * 512:base + half * 512
                                   + 512],
                           scalar=1.0, op0=ALU.mult,
                           in1=qkv[:, base + half * 512:base + half * 512
                                   + 512], op1=ALU.mult),
                       waits=(("sc", sce),), inc="vv")
                p.emit("vector", lambda e, col=col, half=half:
                       e.tensor_reduce(
                           out=l2q[:, col + half * 4:col + half * 4 + 4],
                           in_=sqA[:].rearrange("p (h d) -> p h d", d=128),
                           axis=AX.X, op=ALU.add), inc="vv")
            vvl = p.val("vv")
            p.emit("scalar", lambda e, col=col: e.activation(
                out=l2q[:, col:col + 8], in_=l2q[:, col:col + 8],
                func=AF.Ln, bias=bias_ap(1e-6), scale=1.0),
                waits=(("vvq" if False else "vv", vvl),), inc="sc")
            p.emit("scalar", lambda e, col=col, logb=logb: e.activation(
                out=l2q[:, col:col + 8], in_=l2q[:, col:col + 8],
                func=AF.Exp, bias=bias_ap(logb), scale=-0.5), inc="sc")
            scl = p.val("sc")
            p.emit("vector", lambda e, base=base, col=col: e.tensor_tensor(
                out=qkv[:, base:base + D].rearrange("p (h d) -> p h d",
                                                    d=128),
                in0=qkv[:, base:base + D].rearrange("p (h d) -> p h d",
                                                    d=128),
                in1=l2q[:, col:col + 8].unsqueeze(2).broadcast_to(
                    [128, 8, 128]), op=ALU.mult),
                waits=(("sc", scl),), inc="vv")
        vvqk = p.val("vv")
        for hds in range(H):
            for tens, base in ((A["skq"], 0), (A["skk"], D),
                               (A["skv"], 2 * D)):
                sbsrc = qkv[:, base + hds * 128:base + (hds + 1) * 128]
                p.idma_scatter(tens.rearrange("h s d -> (h s) d"),
                               k.idx_sc[:, i * 8 + hds:i * 8 + hds + 1],
                               sbsrc,
                               waits=(("vv", vvqk), ("ldix", 64)),
                               inc="gsc")
        store_wm[sl] = p.val("vv")
        qkv_gpd_wm = p.val("gsc")

    p.all_sync(eng_waits={"gpsimd": [("gsc", p.val("gsc")),
                                     ("ghd", p.val("ghd")),
                                     ("ggt", p.val("ggt"))]})
    tile_stack.close()
    sb = k.mksb(k._ph1_stack)
    wfi = k.q_Y
    wfo = sb("wfq_out", [128, DFF], FP8)
    first = True
    for (src_t, dst, icol0, nrt, ncb) in (
            (A["w1th"], A["w18s"], 0, 4, 8),
            (A["w2th"], A["w28s"], 4, 16, 2)):
        for rt in range(nrt):
            for cb in range(ncb):
                wvv = () if first else (("vv", p.val("vv")),
                                        ("sc", p.val("sc")))
                if cb == 0:
                    wvv = wvv + (("wfs", p.val("wfs")),)
                first = False
                lv = p.dma("sync", wfi[:],
                           src_t[rt * 128:(rt + 1) * 128,
                                 cb * 512:(cb + 1) * 512],
                           waits=wvv, inc="wfl")
                emit_fq_chain(k, wfi[:], wfo[:, cb * 512:(cb + 1) * 512],
                              2, -14, wv=(("wfl", lv),))
            vvq = p.val("vv")
            ncols = ncb * 512
            p.idma_scatter(dst[:, 0:ncols], k.idx_wf[:, icol0 + rt:
                                                     icol0 + rt + 1],
                           wfo[:, 0:ncols],
                           waits=(("vv", vvq), ("ldix", 64)), inc="wfs")
    p.val("wfs")
    p.val("gpd")


def emit_b1b2(k, sb):
    p, bias_ap = k.p, k.bias_ap
    b1r = sb("b1r_sb", [32, 128])
    p.dma("sync", b1r[:], k.A["b1r"][:, :], inc="ld0")
    b2r = sb("b2r_sb", [1, D])
    p.dma("sync", b2r[:], k.A["b2r"][:, :], inc="ld0")
    b2q = sb("b2qrow", [1, D])
    b1q = sb("b1qrow", [32, 128])
    LDB = k.LD0
    emit_fq_chain(k, b1r[:, :], b1q[:, :], 2, -14, wv=(("ld0", LDB),),
                  pn=32, n=128)
    p.emit("tensor", lambda e: e.transpose(
        k.banks[7][:, 0:32], b1q[:, :], k.ident[0:32, 0:32]),
        waits=(("vv", p.val("vv")), ("ld0", LDB)), inc="tm")
    p.emit("vector", lambda e: e.tensor_copy(k.b1c[:], k.banks[7][:, 0:32]),
           waits=(("tm", p.val("tm")),), inc="vv")
    for hh in range(2):
        emit_fq_chain(k, b2r[0:1, hh * 512:(hh + 1) * 512],
                      b2q[0:1, hh * 512:(hh + 1) * 512], 2, -14,
                      wv=(("ld0", LDB),), pn=1, n=512)
    VVB = p.val("vv")
    for hh in range(2):
        p.emit("tensor", lambda e, hh=hh: e.matmul(
            k.banks[6][:, 0:512], k.ones[0:1, :],
            b2q[0:1, hh * 512:(hh + 1) * 512], start=True, stop=True),
            waits=(("vv", VVB), ("binit", p.val("binit")),
                   ("vv", p.val("vv"))), inc="tm")
        p.emit("vector", lambda e, hh=hh: e.tensor_copy(
            k.b2bc[:, hh * 512:(hh + 1) * 512], k.banks[6][:, 0:512]),
            waits=(("tm", p.val("tm")),), inc="vv")



def emit_p2(k, sb):
    p, banks, ident, bias_ap = k.p, k.banks, k.ident, k.bias_ap
    A, masks, ones = k.A, k.masks, k.ones
    bgc_t = [sb(f"bgc{cc}", [128, 8]) for cc in range(NCH)]
    skbg_flat = A["skbg"].rearrange("a b t c -> (a b t) c")
    for cc in range(NCH):
        p.idma_gather(bgc_t[cc][:, :], skbg_flat,
                      k.idx_bg[:, 32 + cc:32 + cc + 1],
                      waits=(("bar", k.BAR0), ("ldix", 64)), inc="bgl")
    DNL0 = p.val("bgl")
    bgT = sb("bgT", [8, S])
    gTs = sb("gTs", [4, S])
    for cc in range(NCH):
        pb = banks[cc % 2]
        pg = banks[2 + cc % 2]
        p.emit("tensor", lambda e, cc=cc, pb=pb: e.transpose(
            pb[0:8, 0:128], bgc_t[cc][:, :], ident[:]),
            waits=(("bgl", DNL0), ("vv", p.val("vv"))),
            inc="tm")
        p.emit("vector", lambda e, cc=cc, pb=pb: e.tensor_copy(
            bgT[:, cc * 128:(cc + 1) * 128], pb[0:8, 0:128]),
            waits=(("tm", p.val("tm")), ("ld0", k.LD0)), inc="vv")
        p.emit("tensor", lambda e, cc=cc, pg=pg: e.transpose(
            pg[0:4, 0:128], bgc_t[cc][:, 4:8], ident[:]),
            waits=(("vv", p.val("vv")),), inc="tm")
        p.emit("vector", lambda e, cc=cc, pg=pg: e.tensor_copy(
            gTs[:, cc * 128:(cc + 1) * 128], pg[0:4, 0:128]),
            waits=(("tm", p.val("tm")),), inc="vv")
    vvbgT = p.val("vv")
    GT = sb("GT", [4, S])
    zc = sb("zc", [4, 1])
    p.emit("gpsimd", lambda e: e.memset(zc[:], 0.0), inc="binit")
    p.emit("vector", lambda e: e.tensor_tensor_scan(
        out=GT[:], data0=zc[:].broadcast_to([4, S]), data1=gTs[:, :],
        initial=0.0, op0=ALU.add, op1=ALU.add),
        waits=(("vv", vvbgT), ("binit", p.val("binit"))), inc="vv")
    p.emit("vector", lambda e: e.tensor_scalar(
        out=k.stat[:, 11:12], in0=k.stat[:, 11:12], scalar1=0.0,
        scalar2=None, op0=ALU.mult), inc="vv")
    VVGT = p.val("vv")

    GB = sb("GB", [4, 128])
    nGB = sb("nGB", [4, 128])
    LA1 = sb("LA1", [2, 4, 128])
    LA2 = sb("LA2", [2, 4, 128])
    p.dma("sync", LA1[1:2, :, :], A["onesin"][:, :, :], inc="binit")
    p.dma("sync", LA2[0:1, :, :], A["onesin"][:, :, :], inc="binit")
    BIN = p.val("binit")
    gcols = sb("gcols", [128, 4])
    gamc = sb("gamc", [128, 4])
    ghcb = sb("ghcb", [128, 4])
    gamCb = sb("gamCb", [128, 4])
    ekb = sb("ekb", [128, 4])

    Kc = [sb(f"dnK{i}", [128, 128]) for i in range(2)]
    Vc = [sb(f"dnV{i}", [128, 128]) for i in range(2)]
    Qc = [sb(f"dnQ{i}", [128, 128]) for i in range(2)]
    Kfm = sb("dnKfm", [128, 128])
    Qfm = sb("dnQfm", [128, 128])
    Eti = sb("dnEti", [128, 128])
    Eit = sb("dnEit", [128, 128])
    MQK = sb("dnMQK", [128, 128])
    Nr = sb("dnNr", [128, 128])
    Ncl = sb("dnNcl", [128, 128])
    SQ = [sb(f"dnSQ{i}", [128, 128]) for i in range(4)]
    UU = sb("dnUU", [128, 256])
    Wfm = sb("dnWfm", [128, 128])
    Ut = sb("dnUt", [128, 128])
    Ot = sb("dnOt", [128, 128])
    Kbar = sb("dnKbar", [128, 128])
    Sst = [sb(f"dnS{h}", [128, 128]) for h in range(HOWN)]
    ostore_wm = [0]

    for cc in range(NCH):
        if cc == 0:
            p.emit("vector", lambda e: e.tensor_copy(GB[:], GT[:, 0:128]),
                   waits=(("vv", VVGT), ("gpd2", p.val("gpd2"))), inc="vv")
        else:
            p.emit("vector", lambda e, cc=cc: e.tensor_scalar(
                out=GB[:], in0=GT[:, cc * 128:(cc + 1) * 128],
                scalar1=GT[:, cc * 128 - 1:cc * 128], scalar2=None,
                op0=ALU.subtract),
                waits=(("gpd2", p.val("gpd2")),), inc="vv")
        k.vgap(2)
        p.emit("vector", lambda e: e.tensor_scalar(
            out=nGB[:], in0=GB[:], scalar1=-1.0, scalar2=None, op0=ALU.mult),
            inc="vv")
        vvgb = p.val("vv")
        p.dma("gpsimd", LA1[0:1, :, :], GB[:],
              waits=(("vv", vvgb), ("binit", BIN), ("tm", p.val("tm"))),
              inc="gpd2")
        p.dma("gpsimd", LA2[1:2, :, :], nGB[:], waits=(("vv", vvgb),),
              inc="gpd2")
        GPD2 = p.val("gpd2")
        p.emit("tensor", lambda e: e.transpose(
            banks[6][:, 0:4], GB[:], ident[0:4, 0:4]),
            waits=(("vv", vvgb), ("vv", p.val("vv")), ("sc", p.val("sc"))),
            inc="tm")
        p.emit("vector", lambda e: e.tensor_copy(gcols[:], banks[6][:, 0:4]),
               waits=(("tm", p.val("tm")),), inc="vv")
        vvgc = p.val("vv")
        p.emit("scalar", lambda e: e.activation(
            out=gamc[:], in_=gcols[:], func=AF.Exp),
            waits=(("vv", vvgc),), inc="sc")
        p.emit("tensor", lambda e: e.matmul(
            banks[7][:, 0:4], k.sel127[:], gcols[:],
            start=True, stop=True),
            waits=(("vv", vvgc), ("sc", p.val("sc")),
                   ("ldsel", 32)), inc="tm")
        p.emit("vector", lambda e: e.tensor_copy(ghcb[:], banks[7][:, 0:4]),
               waits=(("tm", p.val("tm")),), inc="vv")
        vvgh = p.val("vv")
        p.emit("scalar", lambda e: e.activation(
            out=gamCb[:], in_=ghcb[:], func=AF.Exp),
            waits=(("vv", vvgh),), inc="sc")
        k.vgap(2)
        p.emit("vector", lambda e: e.tensor_tensor(
            out=ekb[:], in0=ghcb[:], in1=gcols[:], op=ALU.subtract),
            waits=(("vv", vvgh),), inc="vv")
        vvek = p.val("vv")
        p.emit("scalar", lambda e: e.activation(
            out=ekb[:], in_=ekb[:], func=AF.Exp),
            waits=(("vv", vvek),), inc="sc")
        SCCH = p.val("sc")

        for hh in range(HOWN):
            sl = (cc * HOWN + hh) % 2
            K_, V_, Q_ = Kc[sl], Vc[sl], Qc[sl]
            buf_free = (("vv", p.val("vv")), ("tm", p.val("tm")))
            dncol = k.idx_dn[:, cc * 4 + hh:cc * 4 + hh + 1]
            dsem = "dnl%d" % sl
            lk = p.idma_gather(K_[:],
                               A["skk"].rearrange("h s d -> (h s) d"),
                               dncol,
                               waits=(("bar", k.BAR0),) + buf_free,
                               inc=dsem)
            lv = p.idma_gather(V_[:],
                               A["skv"].rearrange("h s d -> (h s) d"),
                               dncol, inc=dsem)
            lq = p.idma_gather(Q_[:],
                               A["skq"].rearrange("h s d -> (h s) d"),
                               dncol, inc=dsem)
            p.emit("tensor", lambda e, K_=K_: e.transpose(
                banks[0][:, 0:128], K_[:], ident[:]),
                waits=((dsem, lq), ("vv", p.val("vv"))), inc="tm")
            p.emit("vector", lambda e: e.tensor_copy(
                Kfm[:], banks[0][:, 0:128]),
                waits=(("tm", p.val("tm")),), inc="vv")
            p.emit("tensor", lambda e, Q_=Q_: e.transpose(
                banks[1][:, 0:128], Q_[:], ident[:]),
                waits=((dsem, lq), ("vv", p.val("vv"))), inc="tm")
            p.emit("vector", lambda e: e.tensor_copy(
                Qfm[:], banks[1][:, 0:128]),
                waits=(("tm", p.val("tm")),), inc="vv")
            vvfm = p.val("vv")
            p.emit("tensor", lambda e, hh=hh: e.matmul(
                banks[2][:, 0:128], LA1[0:2, hh, :], LA2[0:2, hh, :],
                start=True, stop=True),
                waits=(("gpd2", GPD2), ("vv", p.val("vv"))), inc="tm")
            tl1 = p.val("tm")
            p.emit("tensor", lambda e, hh=hh: e.matmul(
                banks[3][:, 0:128], LA2[0:2, hh, :], LA1[0:2, hh, :],
                start=True, stop=True), inc="tm")
            tl2 = p.val("tm")
            for (Ed, bank, mk, tl) in ((Eti, banks[2], "ti", tl1),
                                       (Eit, banks[3], "it", tl2)):
                p.emit("vector", lambda e, Ed=Ed, bank=bank, mk=mk:
                       e.tensor_tensor(out=Ed[:], in0=bank[:, 0:128],
                                       in1=masks["m_incl_" + mk][:],
                                       op=ALU.mult),
                       waits=(("tm", tl),), inc="vv")
                p.emit("vector", lambda e, Ed=Ed, mk=mk:
                       e.scalar_tensor_tensor(
                           out=Ed[:], in0=masks["m_incl_" + mk][:],
                           scalar=100.0, op0=ALU.mult, in1=Ed[:],
                           op1=ALU.add), inc="vv")
                vvE = p.val("vv")
                p.emit("scalar", lambda e, Ed=Ed: e.activation(
                    out=Ed[:], in_=Ed[:], func=AF.Exp,
                    bias=bias_ap(-100.0), scale=1.0),
                    waits=(("vv", vvE),), inc="sc")
            scE = p.val("sc")
            p.emit("tensor", lambda e: e.matmul(
                banks[4][:, 0:128], Kfm[:], Kfm[:], start=True, stop=True),
                waits=(("vv", vvfm), ("sc", scE)), inc="tm")
            tP = p.val("tm")
            p.emit("tensor", lambda e: e.matmul(
                banks[5][:, 0:128], Kfm[:], Qfm[:], start=True, stop=True),
                inc="tm")
            tQK = p.val("tm")
            p.emit("vector", lambda e: e.tensor_tensor(
                out=MQK[:], in0=banks[5][:, 0:128], in1=Eit[:], op=ALU.mult),
                waits=(("tm", tQK), ("sc", scE)), inc="vv")
            p.emit("tensor", lambda e, hh=hh, cc=cc: e.matmul(
                banks[6][:, 0:128], k.sel_h[:, hh, :],
                bgT[0:8, cc * 128:(cc + 1) * 128],
                start=True, stop=True),
                waits=(("vv", p.val("vv")), ("ldsel", 32)), inc="tm")
            tBB = p.val("tm")
            p.emit("vector", lambda e, cc=cc, hh=hh: e.tensor_scalar(
                out=Nr[:], in0=Eti[:], scalar1=bgc_t[cc][:, hh:hh + 1],
                scalar2=-1.0, op0=ALU.mult, op1=ALU.mult),
                waits=(("sc", scE),), inc="vv")
            p.emit("vector", lambda e: e.tensor_tensor(
                out=Nr[:], in0=Nr[:], in1=masks["m_strict_ti"][:],
                op=ALU.mult), inc="vv")
            p.emit("vector", lambda e: e.tensor_tensor(
                out=Nr[:], in0=Nr[:], in1=banks[4][:, 0:128], op=ALU.mult),
                waits=(("tm", tP),), inc="vv")
            p.emit("vector", lambda e: e.tensor_tensor(
                out=Ncl[:], in0=Eit[:], in1=masks["m_strict_it"][:],
                op=ALU.mult), inc="vv")
            p.emit("vector", lambda e: e.tensor_tensor(
                out=Ncl[:], in0=Ncl[:], in1=banks[4][:, 0:128],
                op=ALU.mult), inc="vv")
            p.emit("vector", lambda e: e.scalar_tensor_tensor(
                out=Ncl[:], in0=banks[6][:, 0:128], scalar=-1.0,
                op0=ALU.mult, in1=Ncl[:], op1=ALU.mult),
                waits=(("tm", tBB),), inc="vv")
            p.emit("vector", lambda e, K_=K_, cc=cc, hh=hh: e.tensor_scalar(
                out=UU[:, 0:128], in0=K_[:],
                scalar1=bgc_t[cc][:, hh:hh + 1], scalar2=gamc[:, hh:hh + 1],
                op0=ALU.mult, op1=ALU.mult),
                waits=(("sc", SCCH), (dsem, lq)), inc="vv")
            p.emit("vector", lambda e, V_=V_, cc=cc, hh=hh: e.tensor_scalar(
                out=UU[:, 128:256], in0=V_[:],
                scalar1=bgc_t[cc][:, hh:hh + 1], scalar2=None, op0=ALU.mult),
                waits=((dsem, lq),), inc="vv")

            cur_r, cur_c = Nr, Ncl
            for j in range(7):
                ab = banks[6 + (j % 2)]
                p.emit("tensor", lambda e, ab=ab, cur_c=cur_c: e.matmul(
                    ab[:, 0:256], cur_c[:], UU[:], start=True, stop=True),
                    waits=(("vv", p.val("vv")),), inc="tm")
                ta = p.val("tm")
                tsr = tsc = None
                if j < 6:
                    nxt_r = SQ[2 * (j % 2)]
                    nxt_c = SQ[2 * (j % 2) + 1]
                    if j < 5:
                        p.emit("tensor",
                               lambda e, cur_c=cur_c, cur_r=cur_r: e.matmul(
                                   banks[4][:, 0:128], cur_c[:], cur_r[:],
                                   start=True, stop=True), inc="tm")
                        tsr = p.val("tm")
                    p.emit("tensor",
                           lambda e, cur_c=cur_c, cur_r=cur_r: e.matmul(
                               banks[5][:, 0:128], cur_r[:], cur_c[:],
                               start=True, stop=True), inc="tm")
                    tsc = p.val("tm")
                p.emit("vector", lambda e, ab=ab: e.tensor_tensor(
                    out=UU[:], in0=UU[:], in1=ab[:, 0:256], op=ALU.add),
                    waits=(("tm", ta),), inc="vv")
                if j < 6:
                    if j < 5:
                        p.emit("vector", lambda e, nxt_r=nxt_r: e.tensor_copy(
                            nxt_r[:], banks[4][:, 0:128]),
                            waits=(("tm", tsr),), inc="vv")
                    p.emit("vector", lambda e, nxt_c=nxt_c: e.tensor_copy(
                        nxt_c[:], banks[5][:, 0:128]),
                        waits=(("tm", tsc),), inc="vv")
                    cur_r, cur_c = nxt_r, nxt_c
            vvU = p.val("vv")
            p.emit("tensor", lambda e: e.transpose(
                banks[0][:, 0:128], UU[:, 0:128], ident[:]),
                waits=(("vv", vvU),), inc="tm")
            p.emit("vector", lambda e: e.tensor_copy(
                Wfm[:], banks[0][:, 0:128]),
                waits=(("tm", p.val("tm")),), inc="vv")
            vvW = p.val("vv")
            S_h = Sst[hh]
            if cc > 0:
                p.emit("tensor", lambda e, S_h=S_h: e.matmul(
                    banks[1][:, 0:128], Wfm[:], S_h[:], start=True,
                    stop=True),
                    waits=(("vv", vvW), ("dnsv", p.val("dnsv"))), inc="tm")
                tWS = p.val("tm")
                p.emit("vector", lambda e: e.tensor_tensor(
                    out=Ut[:], in0=UU[:, 128:256], in1=banks[1][:, 0:128],
                    op=ALU.subtract), waits=(("tm", tWS),), inc="vv")
                p.emit("tensor", lambda e, S_h=S_h: e.matmul(
                    banks[2][:, 0:128], Qfm[:], S_h[:], start=True,
                    stop=True), inc="tm")
                tQS = p.val("tm")
                p.emit("vector", lambda e, hh=hh: e.tensor_scalar(
                    out=Ot[:], in0=banks[2][:, 0:128],
                    scalar1=gamc[:, hh:hh + 1], scalar2=None, op0=ALU.mult),
                    waits=(("tm", tQS), ("gpd", ostore_wm[0])), inc="vv")
            else:
                p.emit("vector", lambda e: e.tensor_copy(
                    Ut[:], UU[:, 128:256]), waits=(("vv", vvU),), inc="vv")
            vvUt = p.val("vv")
            p.emit("tensor", lambda e: e.matmul(
                banks[3][:, 0:128], MQK[:], Ut[:], start=True, stop=True),
                waits=(("vv", vvUt),), inc="tm")
            tMU = p.val("tm")
            if cc > 0:
                p.emit("vector", lambda e: e.tensor_tensor(
                    out=Ot[:], in0=Ot[:], in1=banks[3][:, 0:128],
                    op=ALU.add), waits=(("tm", tMU),), inc="vv")
            else:
                p.emit("vector", lambda e: e.tensor_copy(
                    Ot[:], banks[3][:, 0:128]),
                    waits=(("tm", tMU), ("gpd", ostore_wm[0])), inc="vv")
            vvO = p.val("vv")
            ostore_wm[0] = p.idma_scatter(
                A["sko"].rearrange("h s d -> (h s) d"),
                k.idx_dn[:, cc * 4 + hh:cc * 4 + hh + 1], Ot[:],
                waits=(("vv", vvO),), inc="gpd")
            p.emit("vector", lambda e, K_=K_, hh=hh: e.tensor_scalar(
                out=Kbar[:], in0=K_[:], scalar1=ekb[:, hh:hh + 1],
                scalar2=None, op0=ALU.mult), waits=(("vv", vvUt),),
                inc="vv")
            vvKb = p.val("vv")
            p.emit("tensor", lambda e: e.matmul(
                banks[4][:, 0:128], Kbar[:], Ut[:], start=True, stop=True),
                waits=(("vv", vvKb),), inc="tm")
            tS = p.val("tm")
            if cc > 0:
                p.emit("vector", lambda e, S_h=S_h, hh=hh: e.tensor_scalar(
                    out=S_h[:], in0=S_h[:], scalar1=gamCb[:, hh:hh + 1],
                    scalar2=None, op0=ALU.mult), waits=(("tm", tS),),
                    inc="vv")
                p.emit("vector", lambda e, S_h=S_h: e.tensor_tensor(
                    out=S_h[:], in0=S_h[:], in1=banks[4][:, 0:128],
                    op=ALU.add), inc="dnsv")
            else:
                p.emit("vector", lambda e, S_h=S_h: e.tensor_copy(
                    S_h[:], banks[4][:, 0:128]), waits=(("tm", tS),),
                    inc="dnsv")
            if getattr(k, "p2_dbg", None):
                k.p2_dbg(cc, hh, locals())
    p.val("gpd")



def emit_p3(k, sb, g, mfm):
    p, banks, ident, bias_ap = k.p, k.banks, k.ident, k.bias_ap
    A = k.A
    stat, l2q = k.stat, k.l2q
    sqA = k.q_A

    wo = [sb(f"wo{kk}", [128, D]) for kk in range(8)]
    for kk in range(8):
        p.dma("sync", wo[kk][:], A["wo"][kk * 128:(kk + 1) * 128, :],
              inc="wold")
    WOLD = p.val("wold")

    ot = sb("p3o", [128, D])
    gt = sb("p3g", [128, D])
    htl = sb("p3h", [128, D])
    x1t = sb("p3x1", [128, D])
    ofm = sb("p3ofm", [128, 8, 128])
    sqA = k.q_A
    scgg = A["sc_gate"].rearrange("(n p) d -> n p d", p=128)
    schh = A["sc_h"].rearrange("(n p) d -> n p d", p=128)
    scx1 = A["sc_x1"].rearrange("(n p) d -> n p d", p=128)

    for il in range(NTG):
        i = g * NTG + il
        lo = []
        for hds in range(H):
            v = p.idma_gather(
                ot[:, hds * 128:(hds + 1) * 128],
                A["sko"].rearrange("h s d -> (h s) d"),
                k.idx_sc[:, i * 8 + hds:i * 8 + hds + 1],
                waits=(("bar", k.BAR1), ("vv", p.val("vv")),
                       ("ldix", 64)),
                inc="p3ov")
            lo.append(v)
        lg = p.dma("sync", gt[:], scgg[i, :, :],
                   waits=(("vv", p.val("vv")), ("ggt", p.val("ggt"))),
                   inc="p3gl")
        lh = p.dma("sync", htl[:], schh[i, :, :],
                   waits=(("ghd", p.val("ghd")),), inc="p3gl")
        for half in range(2):
            p.emit("vector", lambda e, half=half: e.scalar_tensor_tensor(
                out=sqA[:], in0=ot[:, half * 512:half * 512 + 512],
                scalar=1.0, op0=ALU.mult,
                in1=ot[:, half * 512:half * 512 + 512], op1=ALU.mult),
                waits=(("p3ov", lo[-1]), ("sc", p.val("sc"))), inc="vv")
            p.emit("vector", lambda e, half=half: e.tensor_reduce(
                out=l2q[:, half * 4:half * 4 + 4],
                in_=sqA[:].rearrange("p (h d) -> p h d", d=128),
                axis=AX.X, op=ALU.add), inc="vv")
        vvo = p.val("vv")
        p.emit("scalar", lambda e: e.activation(
            out=l2q[:, 0:8], in_=l2q[:, 0:8], func=AF.Ln,
            bias=bias_ap(1e-6), scale=_f32(1.0 / DV)),
            waits=(("vv", vvo),), inc="sc")
        p.emit("scalar", lambda e: e.activation(
            out=l2q[:, 0:8], in_=l2q[:, 0:8], func=AF.Exp,
            bias=bias_ap(0.0), scale=-0.5), inc="sc")
        sco = p.val("sc")
        p.emit("vector", lambda e: e.tensor_tensor(
            out=ot[:].rearrange("p (h d) -> p h d", d=128),
            in0=ot[:].rearrange("p (h d) -> p h d", d=128),
            in1=l2q[:, 0:8].unsqueeze(2).broadcast_to([128, 8, 128]),
            op=ALU.mult), waits=(("sc", sco),), inc="vv")
        p.emit("vector", lambda e: e.tensor_tensor(
            out=ot[:].rearrange("p (h d) -> p h d", d=128),
            in0=ot[:].rearrange("p (h d) -> p h d", d=128),
            in1=k.onw_bc[:].unsqueeze(1).broadcast_to([128, 8, 128]),
            op=ALU.mult), inc="vv")
        p.emit("vector", lambda e: e.tensor_tensor(
            out=ot[:], in0=ot[:], in1=gt[:], op=ALU.mult),
            waits=(("p3gl", lh),), inc="vv")
        vvon = p.val("vv")
        for kk in range(8):
            p.emit("tensor", lambda e, kk=kk: e.transpose(
                banks[kk % 2][:, 0:128], ot[:, kk * 128:(kk + 1) * 128],
                ident[:]),
                waits=(("vv", vvon), ("vv", p.val("vv")),
                       ("sc", p.val("sc"))), inc="tm")
            p.emit("vector", lambda e, kk=kk: e.tensor_copy(
                ofm[:, kk, :], banks[kk % 2][:, 0:128]),
                waits=(("tm", p.val("tm")),), inc="vv")
        vvof = p.val("vv")
        for cg in range(2):
            for kk in range(8):
                p.emit("tensor", lambda e, cg=cg, kk=kk: e.matmul(
                    banks[2 + cg][:, 0:512], ofm[:, kk, :],
                    wo[kk][:, cg * 512:(cg + 1) * 512],
                    start=(kk == 0), stop=(kk == 7)),
                    waits=(("vvq2" if False else "vv", vvof),
                           ("wold", WOLD)), inc="tm")
        two = p.val("tm")
        for cg in range(2):
            p.emit("vector", lambda e, cg=cg: e.tensor_tensor(
                out=x1t[:, cg * 512:(cg + 1) * 512],
                in0=banks[2 + cg][:, 0:512],
                in1=htl[:, cg * 512:(cg + 1) * 512], op=ALU.add),
                waits=(("tm", two), ("p3gl", lh)), inc="vv")
        vvx1 = p.val("vv")
        x1sv = p.dma("gpsimd", scx1[i, :, :], x1t[:], waits=(("vv", vvx1),),
                     inc="gx1")
        p.emit("vector", lambda e: e.scalar_tensor_tensor(
            out=k.sq2[:, 0:512], in0=x1t[:, 0:512], scalar=1.0, op0=ALU.mult,
            in1=x1t[:, 0:512], op1=ALU.mult),
            waits=(("sc", p.val("sc")),), inc="vv")
        p.emit("vector", lambda e: e.scalar_tensor_tensor(
            out=k.sq2[:, 512:1024], in0=x1t[:, 512:1024], scalar=1.0,
            op0=ALU.mult, in1=x1t[:, 512:1024], op1=ALU.mult), inc="vv")
        p.emit("vector", lambda e: e.tensor_reduce(
            out=stat[:, 6:7], in_=k.sq2[:], axis=AX.X, op=ALU.add),
            inc="vv")
        vvm = p.val("vv")
        p.emit("scalar", lambda e: e.activation(
            out=stat[:, 6:7], in_=stat[:, 6:7], func=AF.Ln,
            bias=bias_ap(1e-6), scale=_f32(1.0 / D)),
            waits=(("vv", vvm),), inc="sc")
        p.emit("scalar", lambda e: e.activation(
            out=stat[:, 6:7], in_=stat[:, 6:7], func=AF.Exp,
            bias=bias_ap(0.0), scale=-0.5), inc="sc")
        scm = p.val("sc")
        p.emit("vector", lambda e: e.tensor_scalar(
            out=x1t[:], in0=x1t[:], scalar1=stat[:, 6:7], scalar2=None,
            op0=ALU.mult), waits=(("sc", scm), ("gx1", x1sv)), inc="vv")
        vvmh = p.val("vv")
        for kk in range(8):
            p.emit("tensor", lambda e, kk=kk: e.transpose(
                banks[kk % 2][:, 0:128], x1t[:, kk * 128:(kk + 1) * 128],
                ident[:]),
                waits=(("vv", vvmh), ("vv", p.val("vv")),
                       ("sc", p.val("sc"))), inc="tm")
            p.emit("vector", lambda e, kk=kk, il=il: e.tensor_copy(
                mfm[:, kk, il * 128:(il + 1) * 128],
                banks[kk % 2][:, 0:128]),
                waits=(("tm", p.val("tm")),), inc="vv")



def emit_attnffn(k, sb, g, mfm, yacc):
    p, banks = k.p, k.banks
    A = k.A
    wfcP = [sb(f"wfcP{kk}", [128, 1024]) for kk in range(8)]
    wprP = [sb(f"wprP{ff}", [128, 1024]) for ff in range(8)]
    m1 = [sb(f"m1_{ff}", [128, 512]) for ff in range(8)]

    for pj in range(4):
        lw = []
        for kk in range(8):
            v = p.dma("sync", wfcP[kk][:],
                      A["wfc"][kk * 128:(kk + 1) * 128,
                               pj * 1024:(pj + 1) * 1024],
                      waits=(("tm", p.val("tm")),), inc="afl")
            lw.append(v)
        for ff in range(8):
            v = p.dma("sync", wprP[ff][:],
                      A["wproj"][pj * 1024 + ff * 128:
                                 pj * 1024 + (ff + 1) * 128, :],
                      inc="afl")
            lw.append(v)
        AFL = p.val("afl")
        for half in range(2):
            for ff in range(8):
                for kk in range(8):
                    p.emit("tensor", lambda e, ff=ff, kk=kk, half=half:
                           e.matmul(banks[ff][:, 0:512],
                                    wfcP[kk][:, ff * 128:(ff + 1) * 128],
                                    mfm[:, kk, half * 512:(half + 1) * 512],
                                    start=(kk == 0), stop=(kk == 7)),
                           waits=(("afl", AFL), ("vv", p.val("vv")),
                                  ("sc", p.val("sc"))), inc="tm")
            t1 = p.val("tm")
            for ff in range(8):
                p.emit("scalar", lambda e, ff=ff: e.activation(
                    out=m1[ff][:], in_=banks[ff][:, 0:512], func=AF.Relu),
                    waits=(("tm", t1),), inc="sc")
            s1 = p.val("sc")
            for ff in range(8):
                p.emit("vector", lambda e, ff=ff: e.tensor_tensor(
                    out=m1[ff][:], in0=m1[ff][:], in1=m1[ff][:],
                    op=ALU.mult), waits=(("sc", s1),), inc="vv")
            v1 = p.val("vv")
            for tt in range(4):
                il = half * 4 + tt
                for cg in range(2):
                    for ff in range(8):
                        p.emit("tensor", lambda e, tt=tt, cg=cg, ff=ff:
                               e.matmul(
                                   banks[cg][:, 0:512],
                                   m1[ff][:, tt * 128:(tt + 1) * 128],
                                   wprP[ff][:, cg * 512:(cg + 1) * 512],
                                   start=(ff == 0), stop=(ff == 7)),
                               waits=(("vv", v1), ("vv", p.val("vv"))),
                               inc="tm")
                t2 = p.val("tm")
                for cg in range(2):
                    if pj == 0:
                        p.emit("vector", lambda e, il=il, cg=cg:
                               e.tensor_copy(
                                   yacc[il][:, cg * 512:(cg + 1) * 512],
                                   banks[cg][:, 0:512]),
                               waits=(("tm", t2),), inc="vv")
                    else:
                        p.emit("vector", lambda e, il=il, cg=cg:
                               e.tensor_tensor(
                                   out=yacc[il][:, cg * 512:(cg + 1) * 512],
                                   in0=yacc[il][:, cg * 512:(cg + 1) * 512],
                                   in1=banks[cg][:, 0:512], op=ALU.add),
                               waits=(("tm", t2),), inc="vv")



def emit_p4a(k, sb, g, yacc, x8):
    p, banks, ident, bias_ap = k.p, k.banks, k.ident, k.bias_ap
    A = k.A
    stat = k.stat
    sqA = k.q_A
    bc_sb, bc_cb1, bc_ga = k.bc["sbv"], k.bc["cb1"], k.bc["ga"]

    xpt = sb("p4x", [128, D])
    x1l = sb("p4x1", [128, D])
    x2 = sb("p4x2", [128, D])
    h2 = sb("p4h2", [128, D])
    h2f = sb("p4h2f", [128, 8, 128])
    xin_t = A["x_in"].rearrange("(n p) d -> n p d", p=128)
    scx1 = A["sc_x1"].rearrange("(n p) d -> n p d", p=128)
    scx2 = A["sc_x2"].rearrange("(n p) d -> n p d", p=128)
    x2_wm = [0]

    for il in range(NTG):
        i = g * NTG + il
        lx = p.dma("sync", xpt[:], xin_t[i, :, :],
                   waits=(("vv", p.val("vv")),), inc="p4l")
        l1 = p.dma("sync", x1l[:], scx1[i, :, :],
                   waits=(("gx1", p.val("gx1")),), inc="p4l")
        p.emit("vector", lambda e, il=il: e.tensor_tensor(
            out=x2[:], in0=x1l[:], in1=yacc[il][:], op=ALU.add),
            waits=(("p4l", l1), ("gx2", x2_wm[0])), inc="vv")
        p.emit("vector", lambda e: e.tensor_tensor(
            out=x2[:], in0=x2[:], in1=bc_ga[:], op=ALU.mult), inc="vv")
        p.emit("vector", lambda e: e.tensor_tensor(
            out=x2[:], in0=x2[:], in1=xpt[:], op=ALU.add),
            waits=(("p4l", l1),), inc="vv")
        vvx2 = p.val("vv")
        x2_wm[0] = p.dma("gpsimd", scx2[i, :, :], x2[:],
                         waits=(("vv", vvx2),), inc="gx2")
        p.emit("vector", lambda e: e.tensor_reduce(
            out=stat[:, 0:1], in_=x2[:], axis=AX.X, op=ALU.add),
            waits=(("sc", p.val("sc")),), inc="vv")
        p.emit("vector", lambda e: e.scalar_tensor_tensor(
            out=k.sq2[:, 0:512], in0=x2[:, 0:512], scalar=1.0, op0=ALU.mult,
            in1=x2[:, 0:512], op1=ALU.mult), inc="vv")
        p.emit("vector", lambda e: e.scalar_tensor_tensor(
            out=k.sq2[:, 512:1024], in0=x2[:, 512:1024], scalar=1.0,
            op0=ALU.mult, in1=x2[:, 512:1024], op1=ALU.mult), inc="vv")
        p.emit("vector", lambda e: e.tensor_reduce(
            out=stat[:, 1:2], in_=k.sq2[:], axis=AX.X, op=ALU.add),
            inc="vv")
        vvs0 = p.val("vv")
        p.emit("scalar", lambda e: e.activation(
            out=stat[:, 2:3], in_=stat[:, 0:1], func=AF.Identity,
            scale=_f32(1.0 / D)), waits=(("vv", vvs0),), inc="sc")
        p.emit("scalar", lambda e: e.activation(
            out=stat[:, 3:4], in_=stat[:, 2:3], func=AF.Square), inc="sc")
        scs0 = p.val("sc")
        p.emit("vector", lambda e: e.scalar_tensor_tensor(
            out=stat[:, 3:4], in0=stat[:, 1:2], scalar=_f32(1.0 / D),
            op0=ALU.mult, in1=stat[:, 3:4], op1=ALU.subtract),
            waits=(("sc", scs0),), inc="vv")
        vvs = p.val("vv")
        p.emit("scalar", lambda e: e.activation(
            out=stat[:, 4:5], in_=stat[:, 3:4], func=AF.Ln,
            bias=bias_ap(1e-5), scale=1.0), waits=(("vv", vvs),), inc="sc")
        p.emit("scalar", lambda e: e.activation(
            out=stat[:, 4:5], in_=stat[:, 4:5], func=AF.Exp,
            bias=bias_ap(0.0), scale=-0.5), inc="sc")
        scs = p.val("sc")
        p.emit("vector", lambda e: e.tensor_scalar(
            out=h2[:], in0=x2[:], scalar1=stat[:, 2:3], scalar2=stat[:, 4:5],
            op0=ALU.subtract, op1=ALU.mult), waits=(("sc", scs),), inc="vv")
        p.emit("vector", lambda e: e.tensor_tensor(
            out=h2[:], in0=h2[:], in1=bc_cb1[:], op=ALU.mult), inc="vv")
        p.emit("vector", lambda e: e.tensor_tensor(
            out=h2[:], in0=h2[:], in1=bc_sb[:], op=ALU.add), inc="vv")
        vvh2 = p.val("vv")
        for kk in range(8):
            p.emit("tensor", lambda e, kk=kk: e.transpose(
                banks[kk % 2][:, 0:128], h2[:, kk * 128:(kk + 1) * 128],
                ident[:]),
                waits=(("vv", vvh2), ("vv", p.val("vv")),
                       ("sc", p.val("sc"))), inc="tm")
            p.emit("vector", lambda e, kk=kk: e.tensor_copy(
                h2f[:, kk, :], banks[kk % 2][:, 0:128]),
                waits=(("tm", p.val("tm")),), inc="vv")
        vvhf = p.val("vv")
        h2flat = h2f[:, :, :].rearrange("p a b -> p (a b)")
        x8flat = x8[:, il, :, :].rearrange("p a b -> p (a b)")
        for hhx in range(2):
            emit_fq_chain(k, h2flat[:, hhx * 512:(hhx + 1) * 512],
                          x8flat[:, hhx * 512:(hhx + 1) * 512],
                          2, -14, wv=(("vv", vvhf),), n=512)


def emit_p4b(k, sb, g, x8):
    p, banks, ident, bias_ap = k.p, k.banks, k.ident, k.bias_ap
    A = k.A
    bc_gb = k.bc["gb"]
    w18 = [sb(f"w18_{kk}", [128, DFF], FP8) for kk in range(8)]
    for kk in range(8):
        p.dma("sync", w18[kk][:], A["w18s"][kk * 128:(kk + 1) * 128, :],
              waits=(("bar", k.BAR0),), inc="p4wl")
    w28 = [sb(f"w28_{nn}", [128, D], FP8) for nn in range(32)]
    for nn in range(32):
        p.dma("sync", w28[nn][:], A["w28s"][nn * 128:(nn + 1) * 128, :],
              waits=(("bar", k.BAR0),), inc="p4wl")
    P4WL = p.val("p4wl")
    h38 = sb("h38", [128, 32, GRP], FP8)
    for half in range(2):
        for nn in range(32):
            pb = banks[2 + nn % 2]
            for kk in range(8):
                p.emit("tensor", lambda e, nn=nn, kk=kk, half=half, pb=pb:
                       e.matmul(
                           pb[:, 0:512], w18[kk][:, nn * 128:(nn + 1) * 128],
                           x8[:, half * 4:(half + 1) * 4, kk, :],
                           start=(kk == 0), stop=(kk == 7)),
                       waits=(("p4wl", P4WL), ("vv", p.val("vv"))),
                       inc="tm")
            tq = p.val("tm")
            emit_rmfq_chain(k, pb[:, 0:512], k.b1c[:, nn:nn + 1],
                            h38[:, nn, half * 512:(half + 1) * 512],
                            wv=(("tm", tq),))

    mask_t = A["mask8"].rearrange("(n p) d -> n p d", p=128)
    scx2 = A["sc_x2"].rearrange("(n p) d -> n p d", p=128)
    mt = sb("p5mask", [128, D], U8)
    x2l = sb("p5x2", [128, D])
    y2 = sb("p5y2", [128, D])
    outt = sb("p5out", [128, D])
    out_wm = [0]
    for il in range(NTG):
        i = g * NTG + il
        lm = p.dma("sync", mt[:], mask_t[i, :, :],
                   waits=(("vv", p.val("vv")),), inc="p4l")
        l2 = p.dma("sync", x2l[:], scx2[i, :, :],
                   waits=(("gx2", p.val("gx2")),), inc="p4l")
        for cg in range(2):
            pb = banks[4 + cg]
            for nn in range(32):
                p.emit("tensor", lambda e, il=il, cg=cg, nn=nn, pb=pb:
                       e.matmul(
                           pb[:, 0:512],
                           h38[:, nn, il * 128:(il + 1) * 128],
                           w28[nn][:, cg * 512:(cg + 1) * 512],
                           start=(nn == 0), stop=(nn == 31)),
                       waits=(("vv", p.val("vv")), ("p4wl", P4WL)),
                       inc="tm")
            tq2 = p.val("tm")
            p.emit("vector", lambda e, cg=cg, pb=pb: e.tensor_tensor(
                out=y2[:, cg * 512:(cg + 1) * 512], in0=pb[:, 0:512],
                in1=k.b2bc[:, cg * 512:(cg + 1) * 512], op=ALU.add),
                waits=(("tq2x" if False else "tm", tq2),), inc="vv")
            vvy2 = p.val("vv")
            emit_fq_chain(k, y2[:, cg * 512:(cg + 1) * 512],
                          y2[:, cg * 512:(cg + 1) * 512], 7, None,
                          wv=(("vv", vvy2),))
            p.emit("vector", lambda e, cg=cg: e.scalar_tensor_tensor(
                out=y2[:, cg * 512:(cg + 1) * 512],
                in0=y2[:, cg * 512:(cg + 1) * 512], scalar=1.25,
                op0=ALU.mult, in1=mt[:, cg * 512:(cg + 1) * 512],
                op1=ALU.mult), waits=(("p4l", l2),), inc="vv")
            p.emit("vector", lambda e, cg=cg: e.tensor_tensor(
                out=y2[:, cg * 512:(cg + 1) * 512],
                in0=y2[:, cg * 512:(cg + 1) * 512],
                in1=bc_gb[:, cg * 512:(cg + 1) * 512], op=ALU.mult),
                inc="vv")
            p.emit("vector", lambda e, cg=cg: e.tensor_tensor(
                out=outt[:, cg * 512:(cg + 1) * 512],
                in0=y2[:, cg * 512:(cg + 1) * 512],
                in1=x2l[:, cg * 512:(cg + 1) * 512], op=ALU.add),
                waits=(("outd", out_wm[0]), ("p4l", l2)), inc="vv")
        vvout = p.val("vv")
        out_wm[0] = p.dma("gpsimd", A["out"][i * 128:(i + 1) * 128, :],
                          outt[:], waits=(("vv", vvout),), inc="outd")



_CACHE = {}


def _constants():
    if "c" in _CACHE:
        return _CACHE["c"]
    ident = np.eye(128, dtype=np.float32)
    r = np.arange(128)
    m_incl_ti = (r[:, None] >= r[None, :]).astype(np.float32)
    m_strict_ti = (r[:, None] > r[None, :]).astype(np.float32)
    m_incl_it = (r[:, None] <= r[None, :]).astype(np.float32)
    m_strict_it = (r[:, None] < r[None, :]).astype(np.float32)
    def _rotl(x, r):
        return ((x << np.uint32(r)) | (x >> np.uint32(32 - r))).astype(
            np.uint32)

    def _threefry2x32(k1, k2, x1, x2):
        x1 = x1.astype(np.uint32).copy()
        x2 = x2.astype(np.uint32).copy()
        ks0 = np.uint32(k1)
        ks1 = np.uint32(k2)
        ks2 = np.uint32(ks0 ^ ks1 ^ np.uint32(0x1BD11BDA))
        rot = [13, 15, 26, 6, 17, 29, 16, 24]
        x1 += ks0
        x2 += ks1
        ks = [ks0, ks1, ks2]
        for i in range(5):
            for r in (rot[0:4] if i % 2 == 0 else rot[4:8]):
                x1 += x2
                x2 = _rotl(x2, r)
                x2 ^= x1
            x1 += ks[(i + 1) % 3]
            x2 += ks[(i + 2) % 3] + np.uint32(i + 1)
        return x1, x2

    idx = np.arange(B * S * D, dtype=np.uint64)
    hi = (idx >> np.uint64(32)).astype(np.uint32)
    lo = (idx & np.uint64(0xFFFFFFFF)).astype(np.uint32)
    o1, o2 = _threefry2x32(0, 42, hi, lo)
    bits = o1 ^ o2
    fl = ((bits >> np.uint32(9)) | np.uint32(0x3F800000)).view(
        np.float32) - np.float32(1.0)
    keep = (fl < np.float32(0.8)).reshape(B, S, D).astype(np.uint8)
    _CACHE["c"] = (ident, m_incl_ti, m_strict_ti, m_incl_it, m_strict_it,
                   keep)
    return _CACHE["c"]


def kernel(L0_attn_x, L0_attn_Wq, L0_attn_Wk, L0_attn_Wv, L0_attn_Wb,
           L0_attn_Wa, L0_attn_A_log, L0_attn_dt_bias, L0_attn_Wg,
           L0_attn_o_norm_w, L0_attn_Wo, L0_attn_W_fc, L0_attn_W_proj,
           L0_ffn_W1, L0_ffn_b1, L0_ffn_W2, L0_ffn_b2, L0_c, L0_wmod):
    (ident, m_incl_ti, m_strict_ti, m_incl_it, m_strict_it,
     keep) = _constants()
    f32 = np.float32
    x = np.ascontiguousarray(np.asarray(L0_attn_x, dtype=f32))
    wqkvg = np.ascontiguousarray(np.concatenate(
        [np.asarray(L0_attn_Wq, f32), np.asarray(L0_attn_Wk, f32),
         np.asarray(L0_attn_Wv, f32), np.asarray(L0_attn_Wg, f32)], axis=1))
    wba = np.ascontiguousarray(np.concatenate(
        [np.asarray(L0_attn_Wb, f32), np.asarray(L0_attn_Wa, f32)], axis=1))
    alog_in = np.ascontiguousarray(np.asarray(L0_attn_A_log, f32)[None, :])
    dtb_in = np.ascontiguousarray(np.asarray(L0_attn_dt_bias, f32)[None, :])
    onw = np.ascontiguousarray(np.asarray(L0_attn_o_norm_w, f32)[None, :])
    wo = np.ascontiguousarray(np.asarray(L0_attn_Wo, f32))
    wfc = np.ascontiguousarray(np.asarray(L0_attn_W_fc, f32))
    wproj = np.ascontiguousarray(np.asarray(L0_attn_W_proj, f32))
    w1t = np.ascontiguousarray(np.asarray(L0_ffn_W1, f32).T)
    w2t = np.ascontiguousarray(np.asarray(L0_ffn_W2, f32).T)
    b1r = np.ascontiguousarray(np.asarray(L0_ffn_b1, f32).reshape(32, 128))
    b2r = np.ascontiguousarray(np.asarray(L0_ffn_b2, f32)[None, :])
    cv = np.asarray(L0_c, f32)
    wmod = np.ascontiguousarray(np.asarray(L0_wmod, f32))

    if "nc" not in _CACHE:
        _CACHE["nc"] = build_program()
    nc = _CACHE["nc"]

    in_maps = []
    pr = np.arange(128, dtype=np.uint32)
    nonce = int(np.random.default_rng().integers(1, (1 << 30) - 8))
    sel127c = np.zeros((128, 128), np.float32)
    sel127c[127, :] = 1.0
    selhc = np.zeros((8, 4, 128), np.float32)
    for hh in range(4):
        selhc[hh, hh, :] = 1.0
    onesc = np.ones((1, 4, 128), np.float32)
    for core in range(8):
        b = core // 2
        par = core % 2
        sl = slice(par * TPC, (par + 1) * TPC)
        idx_sc = np.zeros((128, 128), np.uint32)
        for i in range(NT):
            for hds in range(H):
                idx_sc[:, i * 8 + hds] = hds * S + par * TPC + i * 128 + pr
        idx_dn = np.zeros((128, 128), np.uint32)
        for cc in range(NCH):
            for hh in range(HOWN):
                idx_dn[:, cc * 4 + hh] = (par * 4 + hh) * S + cc * 128 + pr
        idx_bg = np.zeros((128, 64), np.uint32)
        for i in range(NT):
            for gidx in range(2):
                idx_bg[:, i * 2 + gidx] = ((par * 2 + gidx) * TPC
                                           + i * 128 + pr)
        for cc in range(NCH):
            idx_bg[:, 32 + cc] = (((cc // 16) * 2 + par) * TPC
                                  + (cc % 16) * 128 + pr)
        idx_wf = np.zeros((128, 20), np.uint32)
        for rt in range(4):
            idx_wf[:, rt] = par * 512 + rt * 128 + pr
        for rt in range(16):
            idx_wf[:, 4 + rt] = par * 2048 + rt * 128 + pr
        in_maps.append({
            "x_in": np.ascontiguousarray(x[b, sl, :]),
            "wqkvg": wqkvg, "wba": wba, "alog_in": alog_in,
            "dtb_in": dtb_in, "onw": onw,
            "wo": wo, "wfc": wfc, "wproj": wproj,
            "w1th": np.ascontiguousarray(w1t[par * 512:(par + 1) * 512]),
            "b1r": b1r,
            "w2th": np.ascontiguousarray(w2t[par * 2048:(par + 1) * 2048]),
            "b2r": b2r,
            "cvec": np.ascontiguousarray(cv[b][None, :]),
            "wmod": wmod,
            "mask8": np.ascontiguousarray(keep[b, sl, :]),
            "ident": ident, "m_incl_ti": m_incl_ti,
            "m_strict_ti": m_strict_ti, "m_incl_it": m_incl_it,
            "m_strict_it": m_strict_it,
            "parin": np.array([[par]], dtype=np.int32),
            "noncein": np.array([[nonce]], dtype=np.int32),
            "idx_sc": idx_sc, "idx_dn": idx_dn, "idx_bg": idx_bg,
            "idx_wf": idx_wf,
            "sel127in": sel127c, "selhin": selhc, "onesin": onesc,
        })
    res = run_bass_kernel_spmd(nc, in_maps, core_ids=list(range(8)))
    out = np.empty((B, S, D), dtype=np.float32)
    for core in range(8):
        b = core // 2
        par = core % 2
        out[b, par * TPC:(par + 1) * TPC, :] = np.asarray(
            res.results[core]["out"])
    return out
